# version 08
# speedup vs baseline: 1.6497x; 1.0020x over previous
import contextlib
import numpy as np
import concourse.bass as bass
import concourse.mybir as mybir
from concourse.bass_utils import run_bass_kernel_spmd

F32 = mybir.dt.float32
BF16 = mybir.dt.bfloat16
AF = mybir.ActivationFunctionType
ALU = mybir.AluOpType
AX = mybir.AxisListType

NCORE = 8
T = 2048
NT = 16
D = 1024
KCH = 8
NH = 16
DH = 64
SEQ = 8192
NS5 = 16
NSC = 64
EPS = 1e-5
SCALE = 0.125


def _w(eng, sem, v):
    if v > 0:
        eng.wait_ge(sem, v)


class _MockIns:
    def then_inc(self, *a, **k):
        return self

    def annotate(self, *a, **k):
        return self


_MOCK_INS = _MockIns()


class _MockEng:
    def __getattr__(self, name):
        return lambda *a, **k: _MOCK_INS


_MOCK_ENG = _MockEng()


class FakeBlock:

    ENGS = ("tensor", "vector", "scalar", "gpsimd", "sync")

    def __init__(self, E):
        self.E = E
        self.q = {n: [] for n in self.ENGS}

    def _add(self, name, f):
        self.E.cur = name
        f(_MOCK_ENG)
        self.q[name].append(f)

    def tensor(self, f):
        self._add("tensor", f)

    def vector(self, f):
        self._add("vector", f)

    def scalar(self, f):
        self._add("scalar", f)

    def gpsimd(self, f):
        self._add("gpsimd", f)

    def sync(self, f):
        self._add("sync", f)

    def replay(self, block):
        E = self.E
        E.mode = "play"
        for name in self.ENGS:
            def run(eng, fns=self.q[name], name=name):
                E.cur = name
                for f in fns:
                    f(eng)
            getattr(block, name)(run)


class Env:

    ENG_SEMS = ("ld", "lda", "ldb", "kba", "kbb", "vba", "vbb",
                "sd", "sda", "sdb", "gd", "gda", "gdb",
                "pe", "sv", "ss", "sg", "cc", "bar")

    BAR_SEMS = {"tensor": ("pe",), "vector": ("sv",),
                "scalar": ("ss", "sd", "sda", "sdb"),
                "gpsimd": ("sg", "gd", "gda", "gdb"),
                "sync": ("ld", "lda", "ldb", "kba", "kbb", "vba", "vbb")}

    def __init__(self, nc, stack):
        self.nc = nc
        self.sems = {n: stack.enter_context(nc.semaphore(n)) for n in self.ENG_SEMS}
        self.c = {}
        self.mode = "rec"
        self.cur = None
        self.tape = {n: [] for n in FakeBlock.ENGS}
        self.tidx = {n: 0 for n in FakeBlock.ENGS}

    def n(self, key):
        return self.c.get(key, 0)

    def add(self, key, k=1):
        if self.mode == "play":
            return self.c.get(key, 0)
        self.c[key] = self.c.get(key, 0) + k
        return self.c[key]

    def _pop(self):
        v = self.tape[self.cur][self.tidx[self.cur]]
        self.tidx[self.cur] += 1
        return v

    def op(self, eng, key, f):
        if self.mode == "rec":
            f()
            v = self.add(key)
            self.tape[self.cur].append(v)
            return v
        v = self._pop()
        _w(eng, self.sems[key], v - 1)
        f().then_inc(self.sems[key], 1)
        return v

    def pe_tag(self, ins):
        if self.mode == "rec":
            v = self.add("pe")
            self.tape[self.cur].append(v)
            return v
        v = self._pop()
        ins.then_inc(self.sems["pe"], 1)
        return v

    def dma(self, q, qkey, out, in_, **kw):
        if self.mode == "rec":
            v = self.add(qkey, 16)
            self.tape[self.cur].append(v)
            return v
        v = self._pop()
        q.dma_start(out=out, in_=in_, **kw).then_inc(self.sems[qkey], 16)
        return v

    def barrier(self, block):
        k = self.add("barn") * 5
        vals = {name: [(sn, self.n(sn)) for sn in self.BAR_SEMS[name]]
                for name in FakeBlock.ENGS}
        for name in FakeBlock.ENGS:
            def f(eng, k=k, name=name):
                for sn, v in vals[name]:
                    _w(eng, self.sems[sn], v)
                eng.sem_inc(self.sems["bar"], 1)
                eng.wait_ge(self.sems["bar"], k)
            getattr(block, name)(f)


def _emit_ln(E, block, x_sb, xhat_fm, scr, tp_ps, ident_bf, eps_col, ld_all,
             sg_consts):
    SV, SS = E.sems["sv"], E.sems["ss"]
    lnxc, lnsq, stats, xhat_bf = (scr["lnxc"], scr["lnsq"], scr["stats"],
                                  scr["xhat_bf"])
    k_sq, k_sd, k_xc, k_xh, k_tpm, k_tpc = {}, {}, {}, {}, {}, {}
    for tc in range(NT):
        p = tc % 2

        def v1(v, tc=tc, p=p):
            _w(v, E.sems["ld"], ld_all)
            if tc - 2 in k_sq:
                v.wait_ge(SS, k_sq[tc - 2])
            xt = x_sb[:, 1024 * tc:1024 * (tc + 1)]
            st = stats[:, 4 * p:4 * p + 4]
            E.op(v, "sv", lambda: v.reduce_sum(st[:, 0:1], xt, axis=AX.X))
            E.op(v, "sv", lambda: v.tensor_scalar_mul(st[:, 1:2], st[:, 0:1],
                                                      -1.0 / 1024.0))
            k_xc[tc] = E.op(v, "sv", lambda: v.tensor_scalar_add(
                lnxc[:, 1024 * p:1024 * (p + 1)], xt, st[:, 1:2]))
        block.vector(v1)

        def s1(s, tc=tc, p=p):
            s.wait_ge(SV, k_xc[tc])
            s.wait_ge(E.sems["sg"], sg_consts)
            st = stats[:, 4 * p:4 * p + 4]
            k_sq[tc] = E.op(s, "ss", lambda: s.activation(
                out=lnsq[:, 1024 * p:1024 * (p + 1)],
                in_=lnxc[:, 1024 * p:1024 * (p + 1)],
                func=AF.Square, accum_out=st[:, 2:3]))
            k_sd[tc] = E.op(s, "ss", lambda: s.activation(
                out=st[:, 3:4], in_=st[:, 2:3], func=AF.Sqrt,
                scale=1.0 / 1024.0, bias=eps_col[:, 0:1]))
        block.scalar(s1)

        def v2(v, tc=tc, p=p):
            v.wait_ge(SS, k_sd[tc])
            st = stats[:, 4 * p:4 * p + 4]
            E.op(v, "sv", lambda: v.reciprocal(st[:, 0:1], st[:, 3:4]))
            k_xh[tc] = E.op(v, "sv", lambda: v.tensor_scalar(
                out=xhat_bf[:, 1024 * p:1024 * (p + 1)],
                in0=lnxc[:, 1024 * p:1024 * (p + 1)],
                scalar1=st[:, 0:1], scalar2=None, op0=ALU.mult))
        block.vector(v2)

        for fc in range(KCH):
            ti = tc * KCH + fc

            def t1(t, tc=tc, p=p, fc=fc, ti=ti):
                if fc == 0:
                    t.wait_ge(SV, k_xh[tc])
                    t.wait_ge(E.sems["sg"], sg_consts)
                if ti - 2 in k_tpc:
                    t.wait_ge(SV, k_tpc[ti - 2])
                k_tpm[ti] = E.pe_tag(t.transpose(
                    tp_ps[ti % 2][:],
                    xhat_bf[:, 1024 * p + 128 * fc:1024 * p + 128 * (fc + 1)],
                    ident_bf[:]))
            block.tensor(t1)

            def v3(v, tc=tc, fc=fc, ti=ti):
                v.wait_ge(E.sems["pe"], k_tpm[ti])
                k_tpc[ti] = E.op(v, "sv", lambda: v.tensor_copy(
                    out=xhat_fm[:, 2048 * fc + 128 * tc:
                                2048 * fc + 128 * (tc + 1)],
                    in_=tp_ps[ti % 2][:]))
            block.vector(v3)


def emit_attn(nc, E, block, sh, w, mla, pfx, ld_x):
    (x_sb, ident_bf, ones_row_bf, ones_col_bf, ones_row_f, eps_col,
     sg_consts) = sh
    WQ, WO, BO = w["WQ"], w["WO"], w["BO"]
    if mla:
        BQ, WL, BL = w["BQ"], w["WL"], w["BL"]
        WK, BK, WV, BV = w["WK"], w["BK"], w["WV"], w["BV"]
        nlat, blk_rows = 2, 256
    else:
        WK, WV = w["WK"], w["WV"]
        nlat, blk_rows = KCH, D
    ccin, ccout = w["ccin"], w["ccout"]
    k_dram, v_dram = w["k_dram"], w["v_dram"]

    st0 = contextlib.ExitStack()
    sb = lambda n, shp, dt=F32: st0.enter_context(nc.sbuf_tensor(pfx + n, shp, dt))
    SV, SS, SG, PE, LD, SD, GD = (E.sems["sv"], E.sems["ss"], E.sems["sg"],
                                  E.sems["pe"], E.sems["ld"], E.sems["sd"],
                                  E.sems["gd"])

    o_fm = sb("o_fm", [128, KCH * 2048], BF16)
    xhat_fm = sb("xhat_fm", [128, KCH * 2048], BF16)
    negb_sb = sb("negb_sb", [128, 1], F32)
    zmax = sb("zmax", [1, 4], F32)
    rrow_f = sb("rrow_f", [1, 512], F32)
    rrow_bf = sb("rrow_bf", [1, 1024], BF16)
    rbc_sb = sb("rbc_sb", [64, 512], BF16)
    q2_bf = sb("q2_bf", [128, 512], BF16)
    bo_row = sb("bo_row", [1, 1024], BF16)
    if mla:
        bqs = sb("bqs", [128, 16], F32)
        bk_sb = sb("bk_sb", [128, 8], F32)
        bl_sb = sb("bl_sb", [128, 2], F32)
        bv_row = sb("bv_row", [1, 1024], BF16)

    with st0:
        def p0_sync(sync):
            if mla:
                for h in range(NH):
                    src = BQ[64 * h:64 * (h + 1)].rearrange("(p o) -> p o", o=1)
                    E.dma(sync, "ld", bqs[0:64, h:h + 1], src,
                          allow_slow_non_contiguous=True)
                    E.dma(sync, "ld", bqs[64:128, h:h + 1], src,
                          allow_slow_non_contiguous=True)
                for fc in range(KCH):
                    E.dma(sync, "ld", bk_sb[:, fc:fc + 1],
                          BK[128 * fc:128 * (fc + 1)].rearrange("(p o) -> p o", o=1),
                          allow_slow_non_contiguous=True)
                for fc in range(2):
                    E.dma(sync, "ld", bl_sb[:, fc:fc + 1],
                          BL[128 * fc:128 * (fc + 1)].rearrange("(p o) -> p o", o=1),
                          allow_slow_non_contiguous=True)
        block.sync(p0_sync)
        ld_p0 = E.n("ld")

        def p0_g(g):
            E.dma(g, "gd", bo_row[0:1, :], BO.rearrange("(o f) -> o f", o=1))
            if mla:
                E.dma(g, "gd", bv_row[0:1, :], BV.rearrange("(o f) -> o f", o=1))
        block.gpsimd(p0_g)

        def p0_v(v):
            E.op(v, "sv", lambda: v.memset(zmax[:], 0.0))
            if mla:
                v.wait_ge(LD, ld_p0)
                E.op(v, "sv", lambda: v.tensor_scalar_mul(bqs[:], bqs[:],
                                                          0.5 * SCALE))
        block.vector(p0_v)

        st1 = contextlib.ExitStack()
        scr = {
            "lnxc": st1.enter_context(nc.sbuf_tensor(pfx + "lnxc", [128, 2 * 1024], F32)),
            "lnsq": st1.enter_context(nc.sbuf_tensor(pfx + "lnsq", [128, 2 * 1024], F32)),
            "stats": st1.enter_context(nc.sbuf_tensor(pfx + "stats", [128, 2 * 4], F32)),
            "xhat_bf": st1.enter_context(nc.sbuf_tensor(pfx + "xhat_bf", [128, 2 * 1024], BF16)),
        }
        tp_ps = [st1.enter_context(nc.psum_tensor(pfx + "tp_psA", [128, 128], BF16)),
                 st1.enter_context(nc.psum_tensor(pfx + "tp_psB", [128, 128], BF16))]
        _emit_ln(E, block, x_sb, xhat_fm, scr, tp_ps, ident_bf, eps_col, ld_x,
                 sg_consts)
        E.barrier(block)
        st1.close()

        st2 = contextlib.ExitStack()
        if mla:
            wl_bf = st2.enter_context(nc.sbuf_tensor(pfx + "wl_bf", [128, KCH * 256], BF16))
            latent_fm = st2.enter_context(nc.sbuf_tensor(pfx + "latent_fm", [128, 2 * 2048], BF16))
            latps = st2.enter_context(nc.psum_tensor(pfx + "latps", [128, 2 * 512], F32))

            def p2_g(g):
                for k in range(KCH):
                    E.dma(g, "gd", wl_bf[:, 256 * k:256 * (k + 1)],
                          WL[128 * k:128 * (k + 1), :])
            block.gpsimd(p2_g)
            gd_wl = E.n("gd")

            k_latmm, k_latev = {}, {}
            for i in range(8):
                fc2, t4 = i // 4, i % 4

                def t2(t, i=i, fc2=fc2, t4=t4):
                    t.wait_ge(GD, gd_wl)
                    if i - 2 in k_latev:
                        t.wait_ge(SS, k_latev[i - 2])
                    for k in range(KCH):
                        ins = t.matmul(latps[:, 512 * (i % 2):512 * (i % 2) + 512],
                                       wl_bf[:, 256 * k + 128 * fc2:256 * k + 128 * (fc2 + 1)],
                                       xhat_fm[:, 2048 * k + 512 * t4:2048 * k + 512 * (t4 + 1)],
                                       start=(k == 0), stop=(k == KCH - 1))
                    k_latmm[i] = E.pe_tag(ins)
                block.tensor(t2)

                def s2(s, i=i, fc2=fc2, t4=t4):
                    s.wait_ge(PE, k_latmm[i])
                    k_latev[i] = E.op(s, "ss", lambda: s.activation(
                        out=latent_fm[:, 2048 * fc2 + 512 * t4:2048 * fc2 + 512 * (t4 + 1)],
                        in_=latps[:, 512 * (i % 2):512 * (i % 2) + 512],
                        func=AF.Identity, bias=bl_sb[:, fc2:fc2 + 1], scale=1.0))
                block.scalar(s2)

            def p2_sync(sync):
                sync.wait_ge(SS, k_latev[7])
                for fc2 in range(2):
                    E.dma(sync, "ld", ccin[128 * fc2:128 * (fc2 + 1), :],
                          latent_fm[:, 2048 * fc2:2048 * (fc2 + 1)])
            block.sync(p2_sync)
        else:
            def p2_sync(sync):
                for kc in range(KCH):
                    E.dma(sync, "ld", ccin[128 * kc:128 * (kc + 1), :],
                          xhat_fm[:, 2048 * kc:2048 * (kc + 1)])
            block.sync(p2_sync)

        ld_cc = E.n("ld")

        def p2_cc(g):
            g.wait_ge(LD, ld_cc)
            g.collective_compute(
                "AllGather", ALU.bypass,
                replica_groups=[[0, 1, 2, 3], [4, 5, 6, 7]],
                ins=[ccin.opt()], outs=[ccout.opt()],
            ).then_inc(E.sems["cc"])
            E.add("cc")
        block.gpsimd(p2_cc)
        cc_val = E.n("cc")

        wk_bf = st2.enter_context(nc.sbuf_tensor(pfx + "wk_bf", [128, nlat * 1024], BF16))
        xg_s = st2.enter_context(nc.sbuf_tensor(pfx + "xg_s", [128, 2 * nlat * 512], BF16))
        k_sb = st2.enter_context(nc.sbuf_tensor(pfx + "k_sb", [128, 2 * 512], BF16))
        k2_bf = st2.enter_context(nc.sbuf_tensor(pfx + "k2_bf", [128, 2 * 512], BF16))
        kps = st2.enter_context(nc.psum_tensor(pfx + "kps", [128, 2 * 512], F32))
        zkps = st2.enter_context(nc.psum_tensor(pfx + "zkps", [1, 512], F32))

        def p3_g(g):
            for k in range(nlat):
                E.dma(g, "gd", wk_bf[:, 1024 * k:1024 * (k + 1)],
                      WK[128 * k:128 * (k + 1), :])
        block.gpsimd(p3_g)
        gd_wk = E.n("gd")

        k_kmm, k_kev, k_k2, k_zkm, k_zkv, k_kst = {}, {}, {}, {}, {}, {}
        ld_xg = {}
        for s5 in range(NS5):
            def p3_sync(sync, s5=s5):
                if s5 == 0:
                    sync.wait_ge(E.sems["cc"], cc_val)
                if (s5 - 2) * KCH + KCH - 1 in k_kmm:
                    sync.wait_ge(PE, k_kmm[(s5 - 2) * KCH + KCH - 1])
                for k in range(nlat):
                    ld_xg[s5] = E.dma(
                        sync, "lda" if s5 % 2 == 0 else "ldb",
                        xg_s[:, (s5 % 2) * nlat * 512 + 512 * k:
                             (s5 % 2) * nlat * 512 + 512 * (k + 1)],
                        ccout[blk_rows * (s5 // 4) + 128 * k:
                              blk_rows * (s5 // 4) + 128 * (k + 1),
                              512 * (s5 % 4):512 * (s5 % 4) + 512])
            block.sync(p3_sync)

            for fc in range(KCH):
                j = s5 * KCH + fc

                def t3(t, s5=s5, fc=fc, j=j):
                    t.wait_ge(E.sems["lda" if s5 % 2 == 0 else "ldb"], ld_xg[s5])
                    t.wait_ge(GD, gd_wk)
                    if j - 2 in k_kev:
                        t.wait_ge(SS, k_kev[j - 2])
                    for k in range(nlat):
                        ins = t.matmul(kps[:, 512 * (j % 2):512 * (j % 2) + 512],
                                       wk_bf[:, 1024 * k + 128 * fc:1024 * k + 128 * (fc + 1)],
                                       xg_s[:, (s5 % 2) * nlat * 512 + 512 * k:
                                            (s5 % 2) * nlat * 512 + 512 * (k + 1)],
                                       start=(k == 0), stop=(k == nlat - 1))
                    k_kmm[j] = E.pe_tag(ins)
                block.tensor(t3)

                def s3(s, j=j, fc=fc):
                    s.wait_ge(PE, k_kmm[j])
                    if j - 2 in k_kst:
                        s.wait_ge(E.sems["sda" if j % 2 == 0 else "sdb"],
                                  k_kst[j - 2])
                    if j - 2 in k_k2:
                        s.wait_ge(SV, k_k2[j - 2])
                    if mla:
                        k_kev[j] = E.op(s, "ss", lambda: s.activation(
                            out=k_sb[:, 512 * (j % 2):512 * (j % 2) + 512],
                            in_=kps[:, 512 * (j % 2):512 * (j % 2) + 512],
                            func=AF.Identity, bias=bk_sb[:, fc:fc + 1], scale=1.0))
                    else:
                        k_kev[j] = E.op(s, "ss", lambda: s.copy(
                            out=k_sb[:, 512 * (j % 2):512 * (j % 2) + 512],
                            in_=kps[:, 512 * (j % 2):512 * (j % 2) + 512]))
                block.scalar(s3)

                def v3b(v, j=j):
                    v.wait_ge(SS, k_kev[j])
                    if j - 2 in k_zkm:
                        v.wait_ge(PE, k_zkm[j - 2])
                    k_k2[j] = E.op(v, "sv", lambda: v.tensor_mul(
                        k2_bf[:, 512 * (j % 2):512 * (j % 2) + 512],
                        k_sb[:, 512 * (j % 2):512 * (j % 2) + 512],
                        k_sb[:, 512 * (j % 2):512 * (j % 2) + 512]))
                block.vector(v3b)

                def t3b(t, s5=s5, fc=fc, j=j):
                    t.wait_ge(SV, k_k2[j])
                    t.wait_ge(SG, sg_consts)
                    if fc == 0 and s5 - 1 in k_zkv:
                        t.wait_ge(SV, k_zkv[s5 - 1])
                    k_zkm[j] = E.pe_tag(t.matmul(
                        zkps[0:1, :], ones_col_bf[:, 0:1],
                        k2_bf[:, 512 * (j % 2):512 * (j % 2) + 512],
                        start=(fc == 0), stop=(fc == KCH - 1)))
                block.tensor(t3b)

                def sc3(s, s5=s5, fc=fc, j=j):
                    s.wait_ge(SS, k_kev[j])
                    k_kst[j] = E.dma(s, "sda" if j % 2 == 0 else "sdb",
                                     k_dram[fc, :, 512 * s5:512 * (s5 + 1)],
                                     k_sb[:, 512 * (j % 2):512 * (j % 2) + 512])
                block.scalar(sc3)

            def v3c(v, s5=s5):
                v.wait_ge(PE, k_zkm[s5 * KCH + KCH - 1])
                E.op(v, "sv", lambda: v.reduce_max(zmax[0:1, 2:3], zkps[0:1, :],
                                                   axis=AX.X))
                k_zkv[s5] = E.op(v, "sv", lambda: v.tensor_max(
                    zmax[0:1, 0:1], zmax[0:1, 0:1], zmax[0:1, 2:3]))
            block.vector(v3c)

        wv_bf = wk_bf
        v_sb2 = st2.enter_context(nc.sbuf_tensor(pfx + "v_sb2", [128, 2 * 1024], BF16))
        kmm_last = k_kmm[NS5 * KCH - 1]

        def p4_g(g):
            g.wait_ge(PE, kmm_last)
            for k in range(nlat):
                E.dma(g, "gd", wv_bf[:, 1024 * k:1024 * (k + 1)],
                      WV[128 * k:128 * (k + 1), :])
        block.gpsimd(p4_g)
        gd_wv = E.n("gd")

        k_vmm, k_vev, k_vst = {}, {}, {}
        ld_xg2 = {}
        for s5 in range(NS5):
            def p4_sync(sync, s5=s5):
                if s5 < 2:
                    sync.wait_ge(PE, kmm_last)
                if (s5 - 2) * 8 + 7 in k_vmm:
                    sync.wait_ge(PE, k_vmm[(s5 - 2) * 8 + 7])
                for k in range(nlat):
                    ld_xg2[s5] = E.dma(
                        sync, "lda" if s5 % 2 == 0 else "ldb",
                        xg_s[:, (s5 % 2) * nlat * 512 + 512 * k:
                             (s5 % 2) * nlat * 512 + 512 * (k + 1)],
                        ccout[blk_rows * (s5 // 4) + 128 * k:
                              blk_rows * (s5 // 4) + 128 * (k + 1),
                              512 * (s5 % 4):512 * (s5 % 4) + 512])
            block.sync(p4_sync)

            for tt in range(4):
                vt = s5 * 4 + tt
                for fh in range(2):
                    j = vt * 2 + fh

                    def t4f(t, s5=s5, tt=tt, fh=fh, j=j):
                        t.wait_ge(SG, sg_consts)
                        t.wait_ge(E.sems["lda" if s5 % 2 == 0 else "ldb"], ld_xg2[s5])
                        t.wait_ge(GD, gd_wv)
                        if j - 2 in k_vev:
                            t.wait_ge(SS, k_vev[j - 2])
                        dst = kps[:, 512 * (j % 2):512 * (j % 2) + 512]
                        first = True
                        if mla:
                            t.matmul(dst, ones_row_bf[0:1, 0:128],
                                     bv_row[0:1, 512 * fh:512 * (fh + 1)],
                                     start=True, stop=False)
                            first = False
                        for k in range(nlat):
                            ins = t.matmul(dst,
                                           xg_s[:, (s5 % 2) * nlat * 512 + 512 * k + 128 * tt:
                                                (s5 % 2) * nlat * 512 + 512 * k + 128 * (tt + 1)],
                                           wv_bf[:, 1024 * k + 512 * fh:1024 * k + 512 * (fh + 1)],
                                           start=(first and k == 0), stop=(k == nlat - 1))
                        k_vmm[j] = E.pe_tag(ins)
                    block.tensor(t4f)

                    def s4f(s, vt=vt, fh=fh, j=j):
                        s.wait_ge(PE, k_vmm[j])
                        if vt - 2 in k_vst:
                            s.wait_ge(E.sems["sda" if vt % 2 == 0 else "sdb"],
                                      k_vst[vt - 2])
                        k_vev[j] = E.op(s, "ss", lambda: s.copy(
                            out=v_sb2[:, 1024 * (vt % 2) + 512 * fh:
                                      1024 * (vt % 2) + 512 * (fh + 1)],
                            in_=kps[:, 512 * (j % 2):512 * (j % 2) + 512]))
                    block.scalar(s4f)

                def sc4(s, s5=s5, tt=tt, vt=vt):
                    s.wait_ge(SS, k_vev[vt * 2 + 1])
                    row = 512 * s5 + 128 * tt
                    k_vst[vt] = E.dma(s, "sda" if vt % 2 == 0 else "sdb",
                                      v_dram[row:row + 128, :],
                                      v_sb2[:, 1024 * (vt % 2):1024 * (vt % 2) + 1024])
                block.scalar(sc4)
        E.barrier(block)
        st2.close()

        st5 = contextlib.ExitStack()
        kbuf = st5.enter_context(nc.sbuf_tensor(pfx + "kbuf", [128, 2 * SEQ], BF16))
        q_pair = st5.enter_context(nc.sbuf_tensor(pfx + "q_pair", [128, 2 * 2048], BF16))
        wq_cur = st5.enter_context(nc.sbuf_tensor(pfx + "wq_cur", [128, 2048], BF16))
        vbuf = st5.enter_context(nc.sbuf_tensor(pfx + "vbuf", [128, 2 * 64 * 65], BF16))
        pbuf = st5.enter_context(nc.sbuf_tensor(pfx + "pbuf", [128, 3 * 1024], BF16))
        sps0 = st5.enter_context(nc.psum_tensor(pfx + "sps0", [128, 1024], F32))
        sps1 = st5.enter_context(nc.psum_tensor(pfx + "sps1", [128, 1024], F32))
        ops_ = st5.enter_context(nc.psum_tensor(pfx + "ops_", [128, 1024], F32))
        aux = st5.enter_context(nc.psum_tensor(pfx + "aux", [128, 512], F32))
        bcps = st5.enter_context(nc.psum_tensor(pfx + "bcps", [64, 512], F32))
        sps = [sps0, sps1]

        def p5_init_g(g):
            E.op(g, "sg", lambda: g.memset(vbuf[:], 1.0))
        block.gpsimd(p5_init_g)
        sg_vbuf = E.n("sg")
        sd_kv = (E.n("sda"), E.n("sdb"))

        k_qmm, k_qev, k_q2, k_zq, k_zqv, k_negb = {}, {}, {}, {}, {}, {}
        k_smm, k_exp, k_avd, k_rr, k_bc, k_bcc, k_onorm = ({}, {}, {}, {}, {},
                                                           {}, {})
        k_kdup = {}
        ld_kpair, ld_vt = {}, {}

        def exp_idx(hp, h01, qh, sc):
            return ((hp * 2 + h01) * 2 + qh) * NSC + sc

        def emit_kld(hd):
            def f_kld(g, hd=hd):
                if hd < 2:
                    g.wait_ge(E.sems["sda"], sd_kv[0])
                    g.wait_ge(E.sems["sdb"], sd_kv[1])
                if hd - 2 >= 0:
                    g.wait_ge(PE, k_smm[128 * (hd - 2) + 127])
                base = SEQ * (hd % 2)
                src = k_dram[hd // 2, 64 * (hd % 2):64 * (hd % 2) + 64, :]
                ld_kpair[hd] = (
                    E.dma(g, "kba", kbuf[0:64, base:base + SEQ // 2],
                          src[:, 0:SEQ // 2]),
                    E.dma(g, "kbb", kbuf[0:64, base + SEQ // 2:base + SEQ],
                          src[:, SEQ // 2:SEQ]))
            block.gpsimd(f_kld)

        gd_wq_map = {}

        def emit_wq(hp):
            def f_wq(g, hp=hp):
                if hp - 1 >= 0:
                    g.wait_ge(PE, k_qmm[(hp - 1, 7)])
                for h01 in range(2):
                    dst = wq_cur[:, 1024 * h01:1024 * (h01 + 1)] \
                        .rearrange("p (k f) -> p k f", f=128)
                    src = WQ[:, 128 * hp + 64 * h01:128 * hp + 64 * (h01 + 1)] \
                        .rearrange("(k p) f -> p k f", p=128)
                    E.dma(g, "gda" if hp % 2 == 0 else "gdb", dst[:, :, 0:64], src)
                    E.dma(g, "gda" if hp % 2 == 0 else "gdb", dst[:, :, 64:128], src)
            block.gpsimd(f_wq)
            gd_wq_map[hp] = E.n("gda" if hp % 2 == 0 else "gdb")

        for hp in range(KCH):
            if hp == 0:
                emit_kld(0)
                emit_kld(1)
                emit_wq(0)
            gd_wq = gd_wq_map[hp]

            def f_kdupA(v, hp=hp):
                v.wait_ge(E.sems["kba"], ld_kpair[2 * hp][0])
                v.wait_ge(E.sems["kbb"], ld_kpair[2 * hp][1])
                k_kdup[2 * hp] = E.op(v, "sv", lambda: v.tensor_copy(
                    out=kbuf[64:128, 0:SEQ], in_=kbuf[0:64, 0:SEQ]))
            block.vector(f_kdupA)

            for u in range(8):
                t4, h01 = u // 2, u % 2
                qcol = 2048 * h01 + 512 * t4

                def f_qt(t, hp=hp, t4=t4, h01=h01, u=u, gd_wq=gd_wq):
                    t.wait_ge(E.sems["gda" if hp % 2 == 0 else "gdb"], gd_wq)
                    if (hp, u - 1) in k_zqv:
                        t.wait_ge(SV, k_zqv[(hp, u - 1)])
                    elif hp - 1 in k_negb:
                        t.wait_ge(SV, k_negb[hp - 1])
                    if (hp, u - 1) in k_qev:
                        t.wait_ge(SS, k_qev[(hp, u - 1)])
                    if u == 0 and hp - 1 >= 0:
                        t.wait_ge(SS, k_exp[exp_idx(hp - 1, 1, 1, NSC - 1)])
                    for k in range(KCH):
                        ins = t.matmul(aux[:, 0:512],
                                       wq_cur[:, 1024 * h01 + 128 * k:
                                              1024 * h01 + 128 * (k + 1)],
                                       xhat_fm[:, 2048 * k + 512 * t4:2048 * k + 512 * (t4 + 1)],
                                       start=(k == 0), stop=(k == KCH - 1))
                    k_qmm[(hp, u)] = E.pe_tag(ins)
                block.tensor(f_qt)

                def f_qs(s, hp=hp, h01=h01, u=u, qcol=qcol):
                    s.wait_ge(PE, k_qmm[(hp, u)])
                    bias = bqs[:, 2 * hp + h01:2 * hp + h01 + 1] if mla else 0.0
                    k_qev[(hp, u)] = E.op(s, "ss", lambda: s.activation(
                        out=q_pair[:, qcol:qcol + 512],
                        in_=aux[:, 0:512], func=AF.Identity, bias=bias,
                        scale=0.5 * SCALE))
                block.scalar(f_qs)

                def f_qv(v, hp=hp, u=u, qcol=qcol):
                    v.wait_ge(SS, k_qev[(hp, u)])
                    if (hp, u - 1) in k_zq:
                        v.wait_ge(PE, k_zq[(hp, u - 1)])
                    elif (hp - 1, 7) in k_zq:
                        v.wait_ge(PE, k_zq[(hp - 1, 7)])
                    k_q2[(hp, u)] = E.op(v, "sv", lambda: v.tensor_mul(
                        q2_bf[:],
                        q_pair[:, qcol:qcol + 512],
                        q_pair[:, qcol:qcol + 512]))
                block.vector(f_qv)

                def f_qz(t, hp=hp, u=u):
                    t.wait_ge(SG, sg_consts)
                    t.wait_ge(SV, k_q2[(hp, u)])
                    t.wait_ge(SS, k_qev[(hp, u)])
                    k_zq[(hp, u)] = E.pe_tag(t.matmul(
                        aux[0:1, :], ones_col_bf[:, 0:1], q2_bf[:],
                        start=True, stop=True))
                block.tensor(f_qz)

                def f_qv2(v, hp=hp, u=u):
                    v.wait_ge(PE, k_zq[(hp, u)])
                    E.op(v, "sv", lambda: v.reduce_max(zmax[0:1, 3:4],
                                                       aux[0:1, :], axis=AX.X))
                    k_zqv[(hp, u)] = E.op(v, "sv", lambda: v.tensor_max(
                        zmax[0:1, 1:2], zmax[0:1, 1:2], zmax[0:1, 3:4]))
                block.vector(f_qv2)

            def f_kdupB(v, hp=hp):
                v.wait_ge(E.sems["kba"], ld_kpair[2 * hp + 1][0])
                v.wait_ge(E.sems["kbb"], ld_kpair[2 * hp + 1][1])
                k_kdup[2 * hp + 1] = E.op(v, "sv", lambda: v.tensor_copy(
                    out=kbuf[64:128, SEQ:2 * SEQ], in_=kbuf[0:64, SEQ:2 * SEQ]))
            block.vector(f_kdupB)

            def f_bv(v, hp=hp):
                E.op(v, "sv", lambda: v.tensor_mul(zmax[0:1, 2:3], zmax[0:1, 0:1],
                                                   zmax[0:1, 1:2]))
            block.vector(f_bv)
            sv_prod = E.n("sv")

            def f_bs(s, hp=hp, sv_prod=sv_prod):
                s.wait_ge(SV, sv_prod)
                E.op(s, "ss", lambda: s.activation(out=zmax[0:1, 3:4],
                                                   in_=zmax[0:1, 2:3], func=AF.Sqrt))
            block.scalar(f_bs)
            ss_sqrt = E.n("ss")

            def f_bv2(v, hp=hp, ss_sqrt=ss_sqrt):
                v.wait_ge(SS, ss_sqrt)
                E.op(v, "sv", lambda: v.tensor_scalar(
                    out=zmax[0:1, 2:3], in0=zmax[0:1, 3:4],
                    scalar1=-1.05 * 1.4142135623730951,
                    scalar2=-2.0, op0=ALU.mult, op1=ALU.add))
                E.op(v, "sv", lambda: v.memset(zmax[0:1, 1:2], 0.0))
            block.vector(f_bv2)
            sv_negb = E.n("sv")

            def f_bt(t, hp=hp, sv_negb=sv_negb):
                t.wait_ge(SG, sg_consts)
                t.wait_ge(SV, sv_negb)
                k_zq[(hp, "b")] = E.pe_tag(t.matmul(
                    aux[:, 0:1], ones_row_f[0:1, 0:128], zmax[0:1, 2:3],
                    start=True, stop=True))
            block.tensor(f_bt)

            def f_bv3(v, hp=hp):
                v.wait_ge(PE, k_zq[(hp, "b")])
                k_negb[hp] = E.op(v, "sv", lambda: v.tensor_copy(
                    out=negb_sb[:], in_=aux[:, 0:1]))
            block.vector(f_bv3)

            for h01 in range(2):
                h = 2 * hp + h01
                hd = 2 * hp + h01

                for bv in range(8):
                    def f_vld(sync, h=h, hd=hd, bv=bv):
                        if hd < 2:
                            sync.wait_ge(SG, sg_vbuf)
                            if hd == 1 and bv == 0:
                                pass
                        if hd - 2 >= 0 and bv == 0:
                            sync.wait_ge(PE, k_avd[128 * (hd - 2) + 127])
                        vbase = 4160 * (hd % 2) + 520 * bv
                        dst = vbuf[:, vbase:vbase + 520].rearrange(
                            "p (s e2) -> p s e2", e2=65)[:, :, 0:64]
                        src = v_dram[1024 * bv:1024 * (bv + 1),
                                     64 * h:64 * (h + 1)].rearrange(
                            "(s p) e2 -> p s e2", p=128)
                        ld_vt[(hd, bv)] = E.dma(sync,
                                                "vba" if bv % 2 == 0 else "vbb",
                                                dst, src)
                    block.sync(f_vld)

                for qh in range(2):
                    g_ = hp * 4 + h01 * 2 + qh
                    av_pend = []
                    for sc in range(NSC):
                        e = exp_idx(hp, h01, qh, sc)

                        def f_qk(t, hp=hp, h01=h01, qh=qh, sc=sc, e=e, hd=hd):
                            if e - 2 >= 0:
                                t.wait_ge(SS, k_exp[e - 2])
                            if sc == 0:
                                t.wait_ge(SV, k_negb[hp])
                                t.wait_ge(SV, k_kdup[hd])
                            kb = kbuf[:, SEQ * (hd % 2) + 128 * sc:
                                      SEQ * (hd % 2) + 128 * (sc + 1)]
                            qp = q_pair[:, 2048 * h01 + 1024 * qh:
                                        2048 * h01 + 1024 * (qh + 1)]
                            for half in range(2):
                                ins = t.matmul(sps[e % 2][:, 512 * half:512 * (half + 1)],
                                               kb,
                                               qp[:, 512 * half:512 * (half + 1)],
                                               start=True, stop=True)
                            k_smm[e] = E.pe_tag(ins)
                        block.tensor(f_qk)

                        def f_ex(s, e=e):
                            s.wait_ge(PE, k_smm[e])
                            if e - 3 >= 0:
                                s.wait_ge(PE, k_avd[e - 3])
                            k_exp[e] = E.op(s, "ss", lambda: s.activation(
                                out=pbuf[:, 1024 * (e % 3):1024 * (e % 3) + 1024],
                                in_=sps[e % 2][:], func=AF.Exp,
                                bias=negb_sb[:, 0:1], scale=1.0))
                        block.scalar(f_ex)

                        def f_av(t, sc=sc, e=e, g_=g_, hd=hd, qh=qh):
                            t.wait_ge(SS, k_exp[e])
                            bv = sc // 8
                            if qh == 0 and sc % 8 == 0:
                                t.wait_ge(E.sems["vba" if bv % 2 == 0 else "vbb"],
                                          ld_vt[(hd, bv)])
                            if sc == 0 and g_ - 1 in k_onorm:
                                t.wait_ge(SV, k_onorm[g_ - 1])
                            vsl = 4160 * (hd % 2) + 65 * sc
                            for w2 in range(2):
                                ins = t.matmul(ops_[0:65, 512 * w2:512 * (w2 + 1)],
                                               vbuf[:, vsl:vsl + 65],
                                               pbuf[:, 1024 * (e % 3) + 512 * w2:
                                                    1024 * (e % 3) + 512 * (w2 + 1)],
                                               start=(sc == 0),
                                               stop=(sc == NSC - 1))
                            k_avd[e] = E.pe_tag(ins)
                        av_pend.append(f_av)
                        if len(av_pend) > 1:
                            block.tensor(av_pend.pop(0))
                    block.tensor(av_pend.pop(0))

                    e_last = exp_idx(hp, h01, qh, NSC - 1)

                    def f_rc(v, g_=g_, e_last=e_last):
                        v.wait_ge(PE, k_avd[e_last])
                        for w2 in range(2):
                            E.op(v, "sv", lambda w2=w2: v.reciprocal(
                                rrow_f[0:1, :],
                                ops_[64:65, 512 * w2:512 * (w2 + 1)]))
                            k_rr[g_] = E.op(v, "sv", lambda w2=w2: v.tensor_copy(
                                out=rrow_bf[0:1, 512 * w2:512 * (w2 + 1)],
                                in_=rrow_f[0:1, :]))
                    block.vector(f_rc)

                    col = 2048 * hp + 1024 * qh
                    for w2 in range(2):
                        def f_bc(t, g_=g_, w2=w2):
                            t.wait_ge(SG, sg_consts)
                            t.wait_ge(SV, k_rr[g_])
                            if (g_, w2 - 1) in k_bcc:
                                t.wait_ge(SV, k_bcc[(g_, w2 - 1)])
                            elif (g_ - 1, 1) in k_bcc:
                                t.wait_ge(SV, k_bcc[(g_ - 1, 1)])
                            k_bc[(g_, w2)] = E.pe_tag(t.matmul(
                                bcps[:], ones_row_bf[0:1, 0:64],
                                rrow_bf[0:1, 512 * w2:512 * (w2 + 1)],
                                start=True, stop=True))
                        block.tensor(f_bc)

                        def f_nm(v, g_=g_, hp=hp, h01=h01, qh=qh, w2=w2,
                                 col=col):
                            v.wait_ge(PE, k_bc[(g_, w2)])
                            k_bcc[(g_, w2)] = E.op(v, "sv", lambda: v.tensor_copy(
                                out=rbc_sb[:], in_=bcps[:]))
                            k_onorm[g_] = E.op(v, "sv", lambda: v.tensor_mul(
                                o_fm[64 * h01:64 * h01 + 64,
                                     col + 512 * w2:col + 512 * (w2 + 1)],
                                ops_[0:64, 512 * w2:512 * (w2 + 1)],
                                rbc_sb[:]))
                        block.vector(f_nm)

                if hd + 2 < 2 * KCH:
                    emit_kld(hd + 2)
                if h01 == 0 and hp + 1 < KCH:
                    emit_wq(hp + 1)
        E.barrier(block)
        st5.close()

        st6 = contextlib.ExitStack()
        wo_bf = st6.enter_context(nc.sbuf_tensor(pfx + "wo_bf", [128, KCH * 1024], BF16))
        wops = st6.enter_context(nc.psum_tensor(pfx + "wops", [128, 2 * 512], F32))

        def p6_g(g):
            for k in range(KCH):
                E.dma(g, "gd", wo_bf[:, 1024 * k:1024 * (k + 1)],
                      WO[128 * k:128 * (k + 1), :])
        block.gpsimd(p6_g)
        gd_wo = E.n("gd")

        k_wmm, k_wev = {}, {}
        for tc in range(NT):
            for fh in range(2):
                i = tc * 2 + fh

                def f_wt(t, tc=tc, fh=fh, i=i):
                    t.wait_ge(GD, gd_wo)
                    t.wait_ge(SG, sg_consts)
                    if i - 2 in k_wev:
                        t.wait_ge(SV, k_wev[i - 2])
                    dst = wops[:, 512 * (i % 2):512 * (i % 2) + 512]
                    t.matmul(dst, ones_row_bf[0:1, 0:128],
                             bo_row[0:1, 512 * fh:512 * (fh + 1)],
                             start=True, stop=False)
                    for k in range(KCH):
                        ins = t.matmul(dst,
                                       o_fm[:, 2048 * k + 128 * tc:2048 * k + 128 * (tc + 1)],
                                       wo_bf[:, 1024 * k + 512 * fh:1024 * k + 512 * (fh + 1)],
                                       start=False, stop=(k == KCH - 1))
                    k_wmm[i] = E.pe_tag(ins)
                block.tensor(f_wt)

                def f_wv(v, tc=tc, fh=fh, i=i):
                    v.wait_ge(PE, k_wmm[i])
                    sl = x_sb[:, 1024 * tc + 512 * fh:1024 * tc + 512 * (fh + 1)]
                    k_wev[i] = E.op(v, "sv", lambda: v.tensor_add(
                        sl, sl, wops[:, 512 * (i % 2):512 * (i % 2) + 512]))
                block.vector(f_wv)
        st6.close()


def _shared_p0(nc, E, block, sh, X):
    (x_sb, ident_bf, ones_row_bf, ones_col_bf, ones_row_f, eps_col, _) = sh

    def p0_sync(sync):
        for tc in range(NT):
            E.dma(sync, "ld", x_sb[:, 1024 * tc:1024 * (tc + 1)],
                  X[128 * tc:128 * (tc + 1), :])
    block.sync(p0_sync)
    ld_x = E.n("ld")

    def p0_g(g):
        E.op(g, "sg", lambda: g.memset(ident_bf[:], 0.0))
        E.op(g, "sg", lambda: g.affine_select(
            out=ident_bf[:], in_=ident_bf[:], compare_op=ALU.not_equal,
            fill=1.0, base=0, pattern=[[-1, 128]], channel_multiplier=1))
        E.op(g, "sg", lambda: g.memset(ones_row_bf[:], 1.0))
        E.op(g, "sg", lambda: g.memset(ones_col_bf[:], 1.0))
        E.op(g, "sg", lambda: g.memset(ones_row_f[:], 1.0))
        E.op(g, "sg", lambda: g.memset(eps_col[:], EPS))
    block.gpsimd(p0_g)
    return ld_x, E.n("sg")


def _mk_shared(nc, st0):
    x_sb = st0.enter_context(nc.sbuf_tensor("x_sb", [128, NT * 1024], F32))
    ident_bf = st0.enter_context(nc.sbuf_tensor("ident_bf", [128, 128], BF16))
    ones_row_bf = st0.enter_context(nc.sbuf_tensor("ones_row_bf", [1, 128], BF16))
    ones_col_bf = st0.enter_context(nc.sbuf_tensor("ones_col_bf", [128, 1], BF16))
    ones_row_f = st0.enter_context(nc.sbuf_tensor("ones_row_f", [1, 128], F32))
    eps_col = st0.enter_context(nc.sbuf_tensor("eps_col", [128, 1], F32))
    return [x_sb, ident_bf, ones_row_bf, ones_col_bf, ones_row_f, eps_col, 0]


def _attn_scratch_dram(nc):
    return {
        "k_dram": nc.dram_tensor("k_dram", [KCH, 128, SEQ], BF16).ap(),
        "v_dram": nc.dram_tensor("v_dram", [SEQ, D], BF16).ap(),
    }


def _emit_store(nc, E, block, x_sb, XOUT, waits):
    def p7_sync(sync):
        for w_sem, w_val in waits:
            _w(sync, w_sem, w_val)
        for tc in range(NT):
            E.dma(sync, "ld", XOUT[128 * tc:128 * (tc + 1), :],
                  x_sb[:, 1024 * tc:1024 * (tc + 1)])
        for nm in ("ld", "lda", "ldb", "kba", "kbb", "vba", "vbb"):
            _w(sync, E.sems[nm], E.n(nm))
    block.sync(p7_sync)

    def p7_s(s):
        for nm in ("sd", "sda", "sdb"):
            _w(s, E.sems[nm], E.n(nm))
    block.scalar(p7_s)

    def p7_g(g):
        for nm in ("gd", "gda", "gdb"):
            _w(g, E.sems[nm], E.n(nm))
    block.gpsimd(p7_g)


def build_attn(mla):
    nc = bass.Bass("TRN2", target_bir_lowering=False)
    X = nc.dram_tensor("X", [T, D], F32, kind="ExternalInput").ap()
    XOUT = nc.dram_tensor("XOUT", [T, D], F32, kind="ExternalOutput").ap()
    w = {"WQ": nc.dram_tensor("WQ", [D, D], F32, kind="ExternalInput").ap(),
         "WO": nc.dram_tensor("WO", [D, D], F32, kind="ExternalInput").ap(),
         "BO": nc.dram_tensor("BO", [D], F32, kind="ExternalInput").ap()}
    if mla:
        w["BQ"] = nc.dram_tensor("BQ", [D], F32, kind="ExternalInput").ap()
        w["WL"] = nc.dram_tensor("WL", [D, 256], F32, kind="ExternalInput").ap()
        w["BL"] = nc.dram_tensor("BL", [256], F32, kind="ExternalInput").ap()
        w["WK"] = nc.dram_tensor("WK", [256, D], F32, kind="ExternalInput").ap()
        w["BK"] = nc.dram_tensor("BK", [D], F32, kind="ExternalInput").ap()
        w["WV"] = nc.dram_tensor("WV", [256, D], F32, kind="ExternalInput").ap()
        w["BV"] = nc.dram_tensor("BV", [D], F32, kind="ExternalInput").ap()
        w["ccin"] = nc.dram_tensor("ccin", [256, T], BF16).ap()
        w["ccout"] = nc.dram_tensor("ccout", [4 * 256, T], BF16).ap()
    else:
        w["WK"] = nc.dram_tensor("WK", [D, D], F32, kind="ExternalInput").ap()
        w["WV"] = nc.dram_tensor("WV", [D, D], F32, kind="ExternalInput").ap()
        w["ccin"] = nc.dram_tensor("ccin", [D, T], BF16).ap()
        w["ccout"] = nc.dram_tensor("ccout", [4 * D, T], BF16).ap()
    w.update(_attn_scratch_dram(nc))

    st0 = contextlib.ExitStack()
    sh = _mk_shared(nc, st0)
    E = Env(nc, st0)
    with st0, nc.Block() as block:
        FB = FakeBlock(E)
        ld_x, sg_consts = _shared_p0(nc, E, FB, sh, X)
        sh[6] = sg_consts
        emit_attn(nc, E, FB, sh, w, mla, "a_", ld_x)
        E.barrier(FB)
        _emit_store(nc, E, FB, sh[0], XOUT, [])
        FB.replay(block)
    return nc


def kernel(**kw):
    return kernel_fused(**kw)


def emit_moe(nc, E, block, sh, w, pfx, ld_x):
    (x_sb, ident_bf, ones_row_bf, ones_col_bf, ones_row_f, eps_col,
     sg_consts) = sh
    GWT, GB = w["GWT"], w["GB"]
    WGT, WUT, WDT = w["WGT"], w["WUT"], w["WDT"]
    SGT, SUT, SDT = w["SGT"], w["SUT"], w["SDT"]
    RWOUT = w.get("RWOUT")
    debug_rw = RWOUT is not None

    st0 = contextlib.ExitStack()
    sb = lambda n, shp, dt=F32: st0.enter_context(nc.sbuf_tensor(pfx + n, shp, dt))
    SV, SS, SG, PE, LD, SD, GD = (E.sems["sv"], E.sems["ss"], E.sems["sg"],
                                  E.sems["pe"], E.sems["ld"], E.sems["sd"],
                                  E.sems["gd"])

    xhat_fm = sb("xhat_fm", [128, KCH * 2048], BF16)
    rw_sb = sb("rw_sb", [128, NT * 64], F32)
    ident_f = sb("ident_f", [128, 128], F32)
    gw_sb = sb("gw_sb", [128, KCH * 64], F32)
    gb_row = sb("gb_row", [1, 64], F32)
    gb_bc = sb("gb_bc", [128, 64], F32)

    with st0:
        def p0_sync(sync):
            for k in range(KCH):
                E.dma(sync, "ld", gw_sb[:, 64 * k:64 * (k + 1)],
                      GWT[128 * k:128 * (k + 1), :])
            E.dma(sync, "ld", gb_row[0:1, :], GB.rearrange("(o f) -> o f", o=1))
        block.sync(p0_sync)
        ld_p0 = E.n("ld")

        def p0_g(g):
            E.op(g, "sg", lambda: g.memset(ident_f[:], 0.0))
            E.op(g, "sg", lambda: g.affine_select(
                out=ident_f[:], in_=ident_f[:], compare_op=ALU.not_equal,
                fill=1.0, base=0, pattern=[[-1, 128]], channel_multiplier=1))
        block.gpsimd(p0_g)
        sg_consts2 = E.n("sg")

        st_gb = contextlib.ExitStack()
        gbps = st_gb.enter_context(nc.psum_tensor(pfx + "gbps", [128, 64], F32))

        def p0_t(t):
            t.wait_ge(LD, ld_p0)
            t.wait_ge(SG, sg_consts)
            return E.pe_tag(t.matmul(gbps[:], ones_row_f[0:1, 0:128],
                                     gb_row[0:1, :], start=True, stop=True))
        pe_gb = []
        block.tensor(lambda t: pe_gb.append(p0_t(t)))

        def p0_v(v):
            v.wait_ge(PE, pe_gb[0])
            E.op(v, "sv", lambda: v.tensor_copy(out=gb_bc[:], in_=gbps[:]))
        block.vector(p0_v)
        sv_gb = E.n("sv")

        st1 = contextlib.ExitStack()
        lnxc = st1.enter_context(nc.sbuf_tensor(pfx + "lnxc", [128, 2 * 1024], F32))
        lnsq = st1.enter_context(nc.sbuf_tensor(pfx + "lnsq", [128, 2 * 1024], F32))
        stats = st1.enter_context(nc.sbuf_tensor(pfx + "stats", [128, 2 * 4], F32))
        xhat_bf = st1.enter_context(nc.sbuf_tensor(pfx + "xhat_bf", [128, 2 * 1024], BF16))
        xhat_f = st1.enter_context(nc.sbuf_tensor(pfx + "xhat_f", [128, 1024], F32))
        xfm32 = st1.enter_context(nc.sbuf_tensor(pfx + "xfm32", [128, 1024], F32))
        scores = st1.enter_context(nc.sbuf_tensor(pfx + "scores", [128, 2 * 64], F32))
        rsc = st1.enter_context(nc.sbuf_tensor(pfx + "rsc", [128, 256], F32))
        tpA = st1.enter_context(nc.psum_tensor(pfx + "tpA", [128, 128], BF16))
        tpB = st1.enter_context(nc.psum_tensor(pfx + "tpB", [128, 128], BF16))
        tfA = st1.enter_context(nc.psum_tensor(pfx + "tfA", [128, 128], F32))
        tfB = st1.enter_context(nc.psum_tensor(pfx + "tfB", [128, 128], F32))
        gpsA = st1.enter_context(nc.psum_tensor(pfx + "gpsA", [128, 64], F32))
        gpsB = st1.enter_context(nc.psum_tensor(pfx + "gpsB", [128, 64], F32))
        tp = [tpA, tpB]
        tf = [tfA, tfB]
        gps = [gpsA, gpsB]

        k_sq, k_sd, k_xc, k_xh, k_xf = {}, {}, {}, {}, {}
        k_tpm, k_tpc, k_tfm, k_tfc = {}, {}, {}, {}
        k_gmm, k_sig, k_rw = {}, {}, {}
        for tc in range(NT):
            p = tc % 2

            def v1(v, tc=tc, p=p):
                _w(v, LD, ld_x)
                if tc - 2 in k_sq:
                    v.wait_ge(SS, k_sq[tc - 2])
                xt = x_sb[:, 1024 * tc:1024 * (tc + 1)]
                st = stats[:, 4 * p:4 * p + 4]
                E.op(v, "sv", lambda: v.reduce_sum(st[:, 0:1], xt, axis=AX.X))
                E.op(v, "sv", lambda: v.tensor_scalar_mul(st[:, 1:2], st[:, 0:1],
                                                          -1.0 / 1024.0))
                k_xc[tc] = E.op(v, "sv", lambda: v.tensor_scalar_add(
                    lnxc[:, 1024 * p:1024 * (p + 1)], xt, st[:, 1:2]))
            block.vector(v1)

            def s1(s, tc=tc, p=p):
                s.wait_ge(SV, k_xc[tc])
                s.wait_ge(SG, sg_consts)
                st = stats[:, 4 * p:4 * p + 4]
                k_sq[tc] = E.op(s, "ss", lambda: s.activation(
                    out=lnsq[:, 1024 * p:1024 * (p + 1)],
                    in_=lnxc[:, 1024 * p:1024 * (p + 1)],
                    func=AF.Square, accum_out=st[:, 2:3]))
                k_sd[tc] = E.op(s, "ss", lambda: s.activation(
                    out=st[:, 3:4], in_=st[:, 2:3], func=AF.Sqrt,
                    scale=1.0 / 1024.0, bias=eps_col[:, 0:1]))
            block.scalar(s1)

            def v2(v, tc=tc, p=p):
                v.wait_ge(SS, k_sd[tc])
                if tc - 1 in k_xf:
                    v.wait_ge(PE, k_tfm[tc * 8 - 1])
                st = stats[:, 4 * p:4 * p + 4]
                E.op(v, "sv", lambda: v.reciprocal(st[:, 0:1], st[:, 3:4]))
                k_xh[tc] = E.op(v, "sv", lambda: v.tensor_scalar(
                    out=xhat_bf[:, 1024 * p:1024 * (p + 1)],
                    in0=lnxc[:, 1024 * p:1024 * (p + 1)],
                    scalar1=st[:, 0:1], scalar2=None, op0=ALU.mult))
                k_xf[tc] = E.op(v, "sv", lambda: v.tensor_scalar(
                    out=xhat_f[:], in0=lnxc[:, 1024 * p:1024 * (p + 1)],
                    scalar1=st[:, 0:1], scalar2=None, op0=ALU.mult))
            block.vector(v2)

            for fc in range(KCH):
                ti = tc * KCH + fc

                def t1(t, tc=tc, p=p, fc=fc, ti=ti):
                    if fc == 0:
                        t.wait_ge(SV, k_xf[tc])
                        t.wait_ge(SG, sg_consts2)
                    if ti - 2 in k_tpc:
                        t.wait_ge(SV, k_tpc[ti - 2])
                    k_tpm[ti] = E.pe_tag(t.transpose(
                        tp[ti % 2][:],
                        xhat_bf[:, 1024 * p + 128 * fc:1024 * p + 128 * (fc + 1)],
                        ident_bf[:]))
                    if ti - 2 in k_tfc:
                        t.wait_ge(SV, k_tfc[ti - 2])
                    k_tfm[ti] = E.pe_tag(t.transpose(
                        tf[ti % 2][:],
                        xhat_f[:, 128 * fc:128 * (fc + 1)], ident_f[:]))
                block.tensor(t1)

                def v3(v, tc=tc, fc=fc, ti=ti):
                    v.wait_ge(PE, k_tpm[ti])
                    k_tpc[ti] = E.op(v, "sv", lambda: v.tensor_copy(
                        out=xhat_fm[:, 2048 * fc + 128 * tc:
                                    2048 * fc + 128 * (tc + 1)],
                        in_=tp[ti % 2][:]))
                    v.wait_ge(PE, k_tfm[ti])
                    k_tfc[ti] = E.op(v, "sv", lambda: v.tensor_copy(
                        out=xfm32[:, 128 * fc:128 * (fc + 1)], in_=tf[ti % 2][:]))
                block.vector(v3)

            def t_gate(t, tc=tc, p=p):
                t.wait_ge(LD, ld_p0)
                t.wait_ge(SV, k_tfc[tc * KCH + KCH - 1])
                if tc - 2 in k_sig:
                    t.wait_ge(SS, k_sig[tc - 2])
                for k in range(KCH):
                    ins = t.matmul(gps[p][:], xfm32[:, 128 * k:128 * (k + 1)],
                                   gw_sb[:, 64 * k:64 * (k + 1)],
                                   start=(k == 0), stop=(k == KCH - 1))
                k_gmm[tc] = E.pe_tag(ins)
            block.tensor(t_gate)

            def s_sig(s, tc=tc, p=p):
                s.wait_ge(PE, k_gmm[tc])
                if tc - 2 in k_rw:
                    s.wait_ge(SV, k_rw[tc - 2])
                k_sig[tc] = E.op(s, "ss", lambda: s.activation(
                    out=scores[:, 64 * p:64 * (p + 1)], in_=gps[p][:],
                    func=AF.Sigmoid))
            block.scalar(s_sig)

            def v_route(v, tc=tc, p=p):
                v.wait_ge(SS, k_sig[tc])
                v.wait_ge(SV, sv_gb)
                sco = scores[:, 64 * p:64 * (p + 1)]
                s4c = rsc[:, 0:64]
                masked = rsc[:, 64:128]
                w64 = rsc[:, 128:192]
                gs = rsc[:, 192:200]
                wk8 = rsc[:, 200:208]
                tmp8 = rsc[:, 208:216]
                m1 = rsc[:, 216:217]
                m2 = rsc[:, 217:218]
                tv = rsc[:, 218:219]
                E.op(v, "sv", lambda: v.tensor_add(s4c, sco, gb_bc[:]))
                for g8 in range(8):
                    sl = s4c[:, 8 * g8:8 * (g8 + 1)]
                    E.op(v, "sv", lambda sl=sl: v.reduce_max(m1, sl, axis=AX.X))
                    E.op(v, "sv", lambda sl=sl: v.tensor_scalar(
                        out=tmp8, in0=sl, scalar1=m1, scalar2=None,
                        op0=ALU.is_equal))
                    E.op(v, "sv", lambda sl=sl: v.scalar_tensor_tensor(
                        out=tmp8, in0=tmp8, scalar=-1e30, in1=sl,
                        op0=ALU.mult, op1=ALU.add))
                    E.op(v, "sv", lambda: v.reduce_max(m2, tmp8, axis=AX.X))
                    E.op(v, "sv", lambda g8=g8: v.tensor_add(
                        gs[:, g8:g8 + 1], m1, m2))
                E.op(v, "sv", lambda: v.tensor_copy(out=wk8, in_=gs))
                for _ in range(3):
                    E.op(v, "sv", lambda: v.reduce_max(m1, wk8, axis=AX.X))
                    E.op(v, "sv", lambda: v.tensor_scalar(
                        out=tmp8, in0=wk8, scalar1=m1, scalar2=None,
                        op0=ALU.is_equal))
                    E.op(v, "sv", lambda: v.scalar_tensor_tensor(
                        out=wk8, in0=tmp8, scalar=-1e30, in1=wk8,
                        op0=ALU.mult, op1=ALU.add))
                E.op(v, "sv", lambda: v.reduce_max(tv, wk8, axis=AX.X))
                E.op(v, "sv", lambda: v.tensor_scalar(
                    out=wk8, in0=gs, scalar1=tv, scalar2=None, op0=ALU.is_ge))
                for g8 in range(8):
                    E.op(v, "sv", lambda g8=g8: v.tensor_scalar(
                        out=masked[:, 8 * g8:8 * (g8 + 1)],
                        in0=s4c[:, 8 * g8:8 * (g8 + 1)],
                        scalar1=wk8[:, g8:g8 + 1], scalar2=None, op0=ALU.mult))
                E.op(v, "sv", lambda: v.tensor_copy(out=w64, in_=masked))
                for _ in range(7):
                    E.op(v, "sv", lambda: v.reduce_max(m1, w64, axis=AX.X))
                    E.op(v, "sv", lambda: v.tensor_scalar(
                        out=s4c, in0=w64, scalar1=m1, scalar2=None,
                        op0=ALU.is_equal))
                    E.op(v, "sv", lambda: v.scalar_tensor_tensor(
                        out=w64, in0=s4c, scalar=-1e30, in1=w64,
                        op0=ALU.mult, op1=ALU.add))
                E.op(v, "sv", lambda: v.reduce_max(tv, w64, axis=AX.X))
                E.op(v, "sv", lambda: v.tensor_scalar(
                    out=masked, in0=masked, scalar1=tv, scalar2=None,
                    op0=ALU.is_ge))
                E.op(v, "sv", lambda: v.tensor_mul(w64, sco, masked))
                E.op(v, "sv", lambda: v.reduce_sum(m1, w64, axis=AX.X))
                E.op(v, "sv", lambda: v.tensor_scalar(
                    out=m1, in0=m1, scalar1=1e-20, scalar2=None, op0=ALU.add))
                E.op(v, "sv", lambda: v.reciprocal(m2, m1))
                k_rw[tc] = E.op(v, "sv", lambda: v.tensor_scalar(
                    out=rw_sb[:, 64 * tc:64 * (tc + 1)], in0=w64, scalar1=m2,
                    scalar2=2.5, op0=ALU.mult, op1=ALU.mult))
            block.vector(v_route)
        E.barrier(block)
        st1.close()
        st_gb.close()

        st2 = contextlib.ExitStack()
        sgT = st2.enter_context(nc.sbuf_tensor(pfx + "sgT", [128, KCH * 1408], BF16))
        suT = st2.enter_context(nc.sbuf_tensor(pfx + "suT", [128, KCH * 1408], BF16))
        sdT = st2.enter_context(nc.sbuf_tensor(pfx + "sdT", [128, 11 * 1024], BF16))
        h_fm = st2.enter_context(nc.sbuf_tensor(pfx + "h_fm", [128, 11 * 512], BF16))
        hsw = st2.enter_context(nc.sbuf_tensor(pfx + "hsw", [128, 2 * 512], F32))
        gpsu = [st2.enter_context(nc.psum_tensor(pfx + f"gps{i}", [128, 512], F32))
                for i in range(2)]
        upsu = [st2.enter_context(nc.psum_tensor(pfx + f"ups{i}", [128, 512], F32))
                for i in range(2)]
        dpsu = [st2.enter_context(nc.psum_tensor(pfx + f"dps{i}", [128, 512], F32))
                for i in range(2)]

        def p2_g(g):
            for k in range(KCH):
                E.dma(g, "gd", sgT[:, 1408 * k:1408 * (k + 1)],
                      SGT[128 * k:128 * (k + 1), :])
                E.dma(g, "gd", suT[:, 1408 * k:1408 * (k + 1)],
                      SUT[128 * k:128 * (k + 1), :])
            for ic in range(11):
                E.dma(g, "gd", sdT[:, 1024 * ic:1024 * (ic + 1)],
                      SDT[128 * ic:128 * (ic + 1), :])
        block.gpsimd(p2_g)
        gd_sh = E.n("gd")

        def emit_mlp(tag, nic, iwid, wg_s, wu_s, wd_s, weight_gd, rw_col,
                     k_state):
            k_gu, k_h, k_dmm, k_acc = k_state
            base = tag
            for t4 in range(4):
                for ic in range(nic):
                    rows = 128 if (ic + 1) * 128 <= iwid else iwid - 128 * ic
                    jj = (base, t4, ic)

                    def tg(t, t4=t4, ic=ic, rows=rows, jj=jj):
                        t.wait_ge(GD, weight_gd)
                        pj = (t4 * nic + ic) % 2
                        prev = (base, t4, ic - 2) if ic - 2 >= 0 else \
                               (base, t4 - 1, nic - 2 + ic) if t4 > 0 else None
                        if prev in k_h:
                            t.wait_ge(SV, k_h[prev])
                        for k in range(KCH):
                            insg = t.matmul(
                                gpsu[pj][0:rows, :],
                                wg_s[:, iwid * k + 128 * ic:iwid * k + 128 * ic + rows],
                                xhat_fm[:, 2048 * k + 512 * t4:2048 * k + 512 * (t4 + 1)],
                                start=(k == 0), stop=(k == KCH - 1))
                        for k in range(KCH):
                            insu = t.matmul(
                                upsu[pj][0:rows, :],
                                wu_s[:, iwid * k + 128 * ic:iwid * k + 128 * ic + rows],
                                xhat_fm[:, 2048 * k + 512 * t4:2048 * k + 512 * (t4 + 1)],
                                start=(k == 0), stop=(k == KCH - 1))
                        k_gu[jj] = E.pe_tag(insu)
                    block.tensor(tg)

                    def sg_(s, t4=t4, ic=ic, rows=rows, jj=jj):
                        s.wait_ge(PE, k_gu[jj])
                        pj = (t4 * nic + ic) % 2
                        k_gu[(jj, "s")] = E.op(s, "ss", lambda: s.activation(
                            out=hsw[0:rows, 512 * pj:512 * (pj + 1)],
                            in_=gpsu[pj][0:rows, :], func=AF.Silu))
                    block.scalar(sg_)

                    def vh(v, t4=t4, ic=ic, rows=rows, jj=jj):
                        v.wait_ge(SS, k_gu[(jj, "s")])
                        pj = (t4 * nic + ic) % 2
                        k_h[jj] = E.op(v, "sv", lambda: v.tensor_mul(
                            h_fm[0:rows, 512 * ic:512 * (ic + 1)],
                            hsw[0:rows, 512 * pj:512 * (pj + 1)],
                            upsu[pj][0:rows, :]))
                    block.vector(vh)

                for tt in range(4):
                    tc = t4 * 4 + tt
                    for fh in range(2):
                        dj = (base, tc, fh)

                        def td(t, t4=t4, tt=tt, tc=tc, fh=fh, dj=dj):
                            pj = (tc * 2 + fh) % 2
                            t.wait_ge(SV, k_h[(base, t4, nic - 1)])
                            prev = (base, tc, fh - 2) if fh - 2 >= 0 else \
                                   (base, tc - 1, fh) if tc > 0 else None
                            if prev in k_acc:
                                t.wait_ge(SV, k_acc[prev])
                            for ic in range(nic):
                                rows = 128 if (ic + 1) * 128 <= iwid else iwid - 128 * ic
                                ins = t.matmul(
                                    dpsu[pj][:],
                                    h_fm[0:rows, 512 * ic + 128 * tt:
                                         512 * ic + 128 * (tt + 1)],
                                    wd_s[0:rows, 1024 * ic + 512 * fh:
                                         1024 * ic + 512 * (fh + 1)],
                                    start=(ic == 0), stop=(ic == nic - 1))
                            k_dmm[dj] = E.pe_tag(ins)
                        block.tensor(td)

                        def vd(v, tc=tc, fh=fh, dj=dj):
                            v.wait_ge(PE, k_dmm[dj])
                            pj = (tc * 2 + fh) % 2
                            sl = x_sb[:, 1024 * tc + 512 * fh:
                                      1024 * tc + 512 * (fh + 1)]
                            if rw_col is None:
                                k_acc[dj] = E.op(v, "sv", lambda: v.tensor_add(
                                    sl, sl, dpsu[pj][:]))
                            else:
                                k_acc[dj] = E.op(v, "sv", lambda: v.scalar_tensor_tensor(
                                    out=sl, in0=dpsu[pj][:],
                                    scalar=rw_sb[:, 64 * tc + rw_col:64 * tc + rw_col + 1],
                                    in1=sl, op0=ALU.mult, op1=ALU.add))
                        block.vector(vd)

        sh_state = ({}, {}, {}, {})
        emit_mlp("sh", 11, 1408, sgT, suT, sdT, gd_sh, None, sh_state)
        E.barrier(block)
        st2.close()

        st3 = contextlib.ExitStack()
        wgb = [st3.enter_context(nc.sbuf_tensor(pfx + f"wg{i}", [128, KCH * 704], BF16))
               for i in range(2)]
        wub = [st3.enter_context(nc.sbuf_tensor(pfx + f"wu{i}", [128, KCH * 704], BF16))
               for i in range(2)]
        wdb = [st3.enter_context(nc.sbuf_tensor(pfx + f"wd{i}", [128, 6 * 1024], BF16))
               for i in range(2)]
        h_fm3 = st3.enter_context(nc.sbuf_tensor(pfx + "h_fm3", [128, 6 * 512], BF16))
        hsw3 = st3.enter_context(nc.sbuf_tensor(pfx + "hsw3", [128, 2 * 512], F32))
        gps3 = [st3.enter_context(nc.psum_tensor(pfx + f"g3{i}", [128, 512], F32))
                for i in range(2)]
        ups3 = [st3.enter_context(nc.psum_tensor(pfx + f"u3{i}", [128, 512], F32))
                for i in range(2)]
        dps3 = [st3.enter_context(nc.psum_tensor(pfx + f"d3{i}", [128, 512], F32))
                for i in range(2)]
        gd_w = {}
        k_edone = {}

        for e in range(64):
            par = e % 2

            def ge_(g, e=e, par=par):
                if e - 2 >= 0:
                    g.wait_ge(PE, k_edone[e - 2])
                sem = "gda" if par == 0 else "gdb"
                E.dma(g, sem,
                      wgb[par][:].rearrange("p (k i) -> p k i", i=704),
                      WGT[e].rearrange("(k p) i -> p k i", p=128))
                E.dma(g, sem,
                      wub[par][:].rearrange("p (k i) -> p k i", i=704),
                      WUT[e].rearrange("(k p) i -> p k i", p=128))
                E.dma(g, sem,
                      wdb[par][:, 0:5120].rearrange("p (k i) -> p k i", i=1024),
                      WDT[e, 0:640, :].rearrange("(k p) i -> p k i", p=128))
                E.dma(g, sem, wdb[par][0:64, 5120:6144], WDT[e, 640:704, :])
            block.gpsimd(ge_)
            gd_w[e] = E.n("gda" if par == 0 else "gdb")

            k_gu, k_h, k_dmm, k_acc = {}, {}, {}, {}
            for t4 in range(4):
                for ic in range(6):
                    rows = 128 if ic < 5 else 64
                    jj = (e, t4, ic)

                    def tg(t, e=e, par=par, t4=t4, ic=ic, rows=rows, jj=jj,
                           k_gu=k_gu, k_h=k_h):
                        t.wait_ge(E.sems["gda" if par == 0 else "gdb"], gd_w[e])
                        pj = (t4 * 6 + ic) % 2
                        prev = (e, t4, ic - 2) if ic - 2 >= 0 else \
                               ((e, t4 - 1, 4 + ic) if t4 > 0 else None)
                        if prev in k_h:
                            t.wait_ge(SV, k_h[prev])
                        for k in range(KCH):
                            t.matmul(gps3[pj][0:rows, :],
                                     wgb[par][:, 704 * k + 128 * ic:
                                              704 * k + 128 * ic + rows],
                                     xhat_fm[:, 2048 * k + 512 * t4:
                                             2048 * k + 512 * (t4 + 1)],
                                     start=(k == 0), stop=(k == KCH - 1))
                        for k in range(KCH):
                            insu = t.matmul(ups3[pj][0:rows, :],
                                            wub[par][:, 704 * k + 128 * ic:
                                                     704 * k + 128 * ic + rows],
                                            xhat_fm[:, 2048 * k + 512 * t4:
                                                    2048 * k + 512 * (t4 + 1)],
                                            start=(k == 0), stop=(k == KCH - 1))
                        k_gu[jj] = E.pe_tag(insu)
                    block.tensor(tg)

                    def sg_(s, t4=t4, ic=ic, rows=rows, jj=jj, k_gu=k_gu):
                        s.wait_ge(PE, k_gu[jj])
                        pj = (t4 * 6 + ic) % 2
                        k_gu[(jj, "s")] = E.op(s, "ss", lambda: s.activation(
                            out=hsw3[0:rows, 512 * pj:512 * (pj + 1)],
                            in_=gps3[pj][0:rows, :], func=AF.Silu))
                    block.scalar(sg_)

                    def vh(v, t4=t4, ic=ic, rows=rows, jj=jj, k_gu=k_gu,
                           k_h=k_h):
                        v.wait_ge(SS, k_gu[(jj, "s")])
                        pj = (t4 * 6 + ic) % 2
                        k_h[jj] = E.op(v, "sv", lambda: v.tensor_mul(
                            h_fm3[0:rows, 512 * ic:512 * (ic + 1)],
                            hsw3[0:rows, 512 * pj:512 * (pj + 1)],
                            ups3[pj][0:rows, :]))
                    block.vector(vh)

                for tt in range(4):
                    tc = t4 * 4 + tt
                    for fh in range(2):
                        dj = (e, tc, fh)

                        def td(t, e=e, par=par, t4=t4, tt=tt, tc=tc, fh=fh, dj=dj,
                               k_h=k_h, k_acc=k_acc, k_dmm=k_dmm):
                            pj = (tc * 2 + fh) % 2
                            t.wait_ge(SV, k_h[(e, t4, 5)])
                            prev = (e, tc, fh - 2) if fh - 2 >= 0 else \
                                   ((e, tc - 1, fh) if tc > 0 else None)
                            if prev in k_acc:
                                t.wait_ge(SV, k_acc[prev])
                            for ic in range(6):
                                rows = 128 if ic < 5 else 64
                                ins = t.matmul(
                                    dps3[pj][:],
                                    h_fm3[0:rows, 512 * ic + 128 * tt:
                                          512 * ic + 128 * (tt + 1)],
                                    wdb[par][0:rows, 1024 * ic + 512 * fh:
                                             1024 * ic + 512 * (fh + 1)],
                                    start=(ic == 0), stop=(ic == 5))
                            k_dmm[dj] = E.pe_tag(ins)
                            if tc == NT - 1 and fh == 1:
                                k_edone[e] = k_dmm[dj]
                        block.tensor(td)

                        def vd(v, e=e, tc=tc, fh=fh, dj=dj, k_dmm=k_dmm, k_acc=k_acc):
                            v.wait_ge(PE, k_dmm[dj])
                            pj = (tc * 2 + fh) % 2
                            sl = x_sb[:, 1024 * tc + 512 * fh:
                                      1024 * tc + 512 * (fh + 1)]
                            k_acc[dj] = E.op(v, "sv", lambda: v.scalar_tensor_tensor(
                                out=sl, in0=dps3[pj][:],
                                scalar=rw_sb[:, 64 * tc + e:64 * tc + e + 1],
                                in1=sl, op0=ALU.mult, op1=ALU.add))
                        block.vector(vd)

        sv_all = E.n("sv")
        if debug_rw:
            def p4_dbg(sync):
                sync.wait_ge(SV, sv_all)
                for tc in range(NT):
                    E.dma(sync, "ld", RWOUT[128 * tc:128 * (tc + 1), :],
                          rw_sb[:, 64 * tc:64 * (tc + 1)])
            block.sync(p4_dbg)
        st3.close()


def emit_ffn1(nc, E, block, sh, w, pfx, ld_x):
    (x_sb, ident_bf, ones_row_bf, ones_col_bf, ones_row_f, eps_col,
     sg_consts) = sh
    WRT, WKT, WVT = w["WRT"], w["WKT"], w["WVT"]
    MUR, MUK, SEL = w["MUR"], w["MUK"], w["SEL"]
    ccin_h, ccout_h = w["ccin_h"], w["ccout_h"]

    st0 = contextlib.ExitStack()
    sb = lambda n, shp, dt=F32: st0.enter_context(nc.sbuf_tensor(pfx + n, shp, dt))
    SV, SS, SG, PE, LD, SD, GD = (E.sems["sv"], E.sems["ss"], E.sems["sg"],
                                  E.sems["pe"], E.sems["ld"], E.sems["sd"],
                                  E.sems["gd"])

    xhat_fm = sb("xhat_fm", [128, KCH * 2048], BF16)
    xr_fm = sb("xr_fm", [128, KCH * 2048], BF16)
    mur_sb = sb("mur_sb", [128, 8], F32)
    muk_sb = sb("muk_sb", [128, 8], F32)
    sel_row = sb("sel_row", [1, 4], F32)
    selbc = sb("selbc", [128, 4], F32)
    hstack = sb("hstack", [128, 32], BF16)
    halo_bf = sb("halo_bf", [128, 2 * 8], BF16)
    haloacc = sb("haloacc", [128, 1], F32)

    with st0:
        def p0_sync(sync):
            for fc in range(KCH):
                E.dma(sync, "ld", mur_sb[:, fc:fc + 1],
                      MUR[128 * fc:128 * (fc + 1)].rearrange("(p o) -> p o", o=1),
                      allow_slow_non_contiguous=True)
                E.dma(sync, "ld", muk_sb[:, fc:fc + 1],
                      MUK[128 * fc:128 * (fc + 1)].rearrange("(p o) -> p o", o=1),
                      allow_slow_non_contiguous=True)
            E.dma(sync, "ld", sel_row[0:1, :], SEL.rearrange("(o f) -> o f", o=1))
        block.sync(p0_sync)
        ld_p0 = E.n("ld")

        st1 = contextlib.ExitStack()
        scr = {
            "lnxc": st1.enter_context(nc.sbuf_tensor(pfx + "lnxc", [128, 2 * 1024], F32)),
            "lnsq": st1.enter_context(nc.sbuf_tensor(pfx + "lnsq", [128, 2 * 1024], F32)),
            "stats": st1.enter_context(nc.sbuf_tensor(pfx + "stats", [128, 2 * 4], F32)),
            "xhat_bf": st1.enter_context(nc.sbuf_tensor(pfx + "xhat_bf", [128, 2 * 1024], BF16)),
        }
        tp_ps = [st1.enter_context(nc.psum_tensor(pfx + "tp_psA", [128, 128], BF16)),
                 st1.enter_context(nc.psum_tensor(pfx + "tp_psB", [128, 128], BF16))]
        _emit_ln(E, block, x_sb, xhat_fm, scr, tp_ps, ident_bf, eps_col, ld_x,
                 sg_consts)
        E.barrier(block)
        st1.close()

        st2 = contextlib.ExitStack()
        selps = st2.enter_context(nc.psum_tensor(pfx + "selps", [128, 4], F32))

        def p2_sync(sync):
            for kc in range(KCH):
                E.dma(sync, "ld", ccin_h[128 * kc:128 * (kc + 1), 0:1],
                      xhat_fm[:, 2048 * kc + 2047:2048 * kc + 2048],
                      allow_slow_non_contiguous=True)
        block.sync(p2_sync)

        ld_cc = E.n("ld")

        def p2_cc(g):
            g.wait_ge(LD, ld_cc)
            g.collective_compute(
                "AllGather", ALU.bypass,
                replica_groups=[[0, 1, 2, 3], [4, 5, 6, 7]],
                ins=[ccin_h.opt()], outs=[ccout_h.opt()],
            ).then_inc(E.sems["cc"])
            E.add("cc")
        block.gpsimd(p2_cc)
        cc_val = E.n("cc")

        def p2b_sync(sync):
            sync.wait_ge(E.sems["cc"], cc_val)
            for kc in range(KCH):
                for r in range(4):
                    E.dma(sync, "ld", hstack[:, 4 * kc + r:4 * kc + r + 1],
                          ccout_h[1024 * r + 128 * kc:1024 * r + 128 * (kc + 1), 0:1],
                          allow_slow_non_contiguous=True)
        block.sync(p2b_sync)
        ld_h = E.n("ld")

        def p2_t(t):
            t.wait_ge(LD, ld_p0)
            t.wait_ge(SG, sg_consts)
            return E.pe_tag(t.matmul(selps[:], ones_row_f[0:1, 0:128],
                                     sel_row[0:1, :], start=True, stop=True))
        pe_sel = []
        block.tensor(lambda t: pe_sel.append(p2_t(t)))

        k_halo = {}

        def p2_v(v):
            v.wait_ge(PE, pe_sel[0])
            v.wait_ge(LD, ld_h)
            E.op(v, "sv", lambda: v.tensor_copy(out=selbc[:], in_=selps[:]))
            for kc in range(KCH):
                E.op(v, "sv", lambda kc=kc: v.tensor_scalar(
                    out=haloacc[:], in0=hstack[:, 4 * kc:4 * kc + 1],
                    scalar1=selbc[:, 0:1], scalar2=None, op0=ALU.mult))
                for r in range(1, 4):
                    E.op(v, "sv", lambda kc=kc, r=r: v.scalar_tensor_tensor(
                        out=haloacc[:], in0=hstack[:, 4 * kc + r:4 * kc + r + 1],
                        scalar=selbc[:, r:r + 1], in1=haloacc[:],
                        op0=ALU.mult, op1=ALU.add))
                k_halo[kc] = E.op(v, "sv", lambda kc=kc: v.tensor_copy(
                    out=halo_bf[:, kc:kc + 1], in_=haloacc[:]))
        block.vector(p2_v)

        st3 = contextlib.ExitStack()
        xk_fm = st3.enter_context(nc.sbuf_tensor(pfx + "xk_fm", [128, KCH * 2048], BF16))
        dbuf = st3.enter_context(nc.sbuf_tensor(pfx + "dbuf", [128, 2048], BF16))
        k_xr = {}

        def p3_v(v):
            for kc in range(KCH):
                xs = xhat_fm[:, 2048 * kc:2048 * (kc + 1)]
                E.op(v, "sv", lambda kc=kc, xs=xs: v.tensor_sub(
                    dbuf[:, 1:2048], xs[:, 1:2048], xs[:, 0:2047]))
                E.op(v, "sv", lambda kc=kc, xs=xs: v.tensor_sub(
                    dbuf[:, 0:1], xs[:, 0:1], halo_bf[:, kc:kc + 1]))
                E.op(v, "sv", lambda kc=kc, xs=xs: v.scalar_tensor_tensor(
                    out=xr_fm[:, 2048 * kc + 1:2048 * (kc + 1)],
                    in0=dbuf[:, 1:2048], scalar=mur_sb[:, kc:kc + 1],
                    in1=xs[:, 0:2047], op0=ALU.mult, op1=ALU.add))
                E.op(v, "sv", lambda kc=kc: v.scalar_tensor_tensor(
                    out=xr_fm[:, 2048 * kc:2048 * kc + 1],
                    in0=dbuf[:, 0:1], scalar=mur_sb[:, kc:kc + 1],
                    in1=halo_bf[:, kc:kc + 1], op0=ALU.mult, op1=ALU.add))
                E.op(v, "sv", lambda kc=kc, xs=xs: v.scalar_tensor_tensor(
                    out=xk_fm[:, 2048 * kc + 1:2048 * (kc + 1)],
                    in0=dbuf[:, 1:2048], scalar=muk_sb[:, kc:kc + 1],
                    in1=xs[:, 0:2047], op0=ALU.mult, op1=ALU.add))
                k_xr[kc] = E.op(v, "sv", lambda kc=kc: v.scalar_tensor_tensor(
                    out=xk_fm[:, 2048 * kc:2048 * kc + 1],
                    in0=dbuf[:, 0:1], scalar=muk_sb[:, kc:kc + 1],
                    in1=halo_bf[:, kc:kc + 1], op0=ALU.mult, op1=ALU.add))
        block.vector(p3_v)
        E.barrier(block)

        wkT = st3.enter_context(nc.sbuf_tensor(pfx + "wkT", [128, KCH * 1024], BF16))
        rel_f = st3.enter_context(nc.sbuf_tensor(pfx + "rel_f", [128, 2 * 512], F32))
        kps4 = [st3.enter_context(nc.psum_tensor(pfx + f"kps4{i}", [128, 512], F32))
                for i in range(2)]
        kact = xhat_fm

        def p4_g(g):
            for k in range(KCH):
                E.dma(g, "gd", wkT[:, 1024 * k:1024 * (k + 1)],
                      WKT[128 * k:128 * (k + 1), :])
        block.gpsimd(p4_g)
        gd_wk = E.n("gd")

        k_kmm, k_rel, k_ka = {}, {}, {}
        for fc in range(KCH):
            for t4 in range(4):
                j = fc * 4 + t4

                def t4a(t, fc=fc, t4=t4, j=j):
                    t.wait_ge(GD, gd_wk)
                    if j - 2 in k_ka:
                        t.wait_ge(SS, k_ka[j - 2])
                    for k in range(KCH):
                        ins = t.matmul(
                            kps4[j % 2][:],
                            wkT[:, 1024 * k + 128 * fc:1024 * k + 128 * (fc + 1)],
                            xk_fm[:, 2048 * k + 512 * t4:2048 * k + 512 * (t4 + 1)],
                            start=(k == 0), stop=(k == KCH - 1))
                    k_kmm[j] = E.pe_tag(ins)
                block.tensor(t4a)

                def v4a(v, j=j):
                    v.wait_ge(PE, k_kmm[j])
                    if j - 2 in k_ka:
                        v.wait_ge(SS, k_ka[j - 2])
                    k_rel[j] = E.op(v, "sv", lambda: v.tensor_scalar_max(
                        rel_f[:, 512 * (j % 2):512 * (j % 2) + 512],
                        kps4[j % 2][:], 0.0))
                block.vector(v4a)

                def s4a(s, fc=fc, t4=t4, j=j):
                    s.wait_ge(SV, k_rel[j])
                    k_ka[j] = E.op(s, "ss", lambda: s.activation(
                        out=kact[:, 2048 * fc + 512 * t4:2048 * fc + 512 * (t4 + 1)],
                        in_=rel_f[:, 512 * (j % 2):512 * (j % 2) + 512],
                        func=AF.Square))
                block.scalar(s4a)
        E.barrier(block)
        st3.close()

        st5 = contextlib.ExitStack()
        wrT = st5.enter_context(nc.sbuf_tensor(pfx + "wrT", [128, KCH * 1024], BF16))
        wvT = st5.enter_context(nc.sbuf_tensor(pfx + "wvT", [128, KCH * 1024], BF16))
        r_bf = st5.enter_context(nc.sbuf_tensor(pfx + "r_bf", [128, 1024], BF16))
        t_f = st5.enter_context(nc.sbuf_tensor(pfx + "t_f", [128, 1024], F32))
        rps = st5.enter_context(nc.psum_tensor(pfx + "rps", [128, 1024], F32))
        kvps = st5.enter_context(nc.psum_tensor(pfx + "kvps", [128, 1024], F32))

        def p5_g(g):
            for k in range(KCH):
                E.dma(g, "gd", wrT[:, 1024 * k:1024 * (k + 1)],
                      WRT[128 * k:128 * (k + 1), :])
            for k in range(KCH):
                E.dma(g, "gd", wvT[:, 1024 * k:1024 * (k + 1)],
                      WVT[128 * k:128 * (k + 1), :])
        block.gpsimd(p5_g)
        gd_rv = E.n("gd")

        k_rmm, k_kvmm, k_sigr, k_mul, k_add = {}, {}, {}, {}, {}
        for tc in range(NT):
            def t5(t, tc=tc):
                t.wait_ge(GD, gd_rv)
                if tc - 1 in k_sigr:
                    t.wait_ge(SS, k_sigr[tc - 1])
                if tc - 1 in k_mul:
                    t.wait_ge(SV, k_mul[tc - 1])
                for fh in range(2):
                    for k in range(KCH):
                        ins = t.matmul(
                            rps[:, 512 * fh:512 * (fh + 1)],
                            xr_fm[:, 2048 * k + 128 * tc:2048 * k + 128 * (tc + 1)],
                            wrT[:, 1024 * k + 512 * fh:1024 * k + 512 * (fh + 1)],
                            start=(k == 0), stop=(k == KCH - 1))
                k_rmm[tc] = E.pe_tag(ins)
                for fh in range(2):
                    for k in range(KCH):
                        ins = t.matmul(
                            kvps[:, 512 * fh:512 * (fh + 1)],
                            kact[:, 2048 * k + 128 * tc:2048 * k + 128 * (tc + 1)],
                            wvT[:, 1024 * k + 512 * fh:1024 * k + 512 * (fh + 1)],
                            start=(k == 0), stop=(k == KCH - 1))
                k_kvmm[tc] = E.pe_tag(ins)
            block.tensor(t5)

            def s5(s, tc=tc):
                s.wait_ge(PE, k_rmm[tc])
                if tc - 1 in k_mul:
                    s.wait_ge(SV, k_mul[tc - 1])
                k_sigr[tc] = E.op(s, "ss", lambda: s.activation(
                    out=r_bf[:], in_=rps[:], func=AF.Sigmoid))
            block.scalar(s5)

            def v5(v, tc=tc):
                v.wait_ge(PE, k_kvmm[tc])
                v.wait_ge(SS, k_sigr[tc])
                k_mul[tc] = E.op(v, "sv", lambda: v.tensor_mul(
                    t_f[:], r_bf[:], kvps[:]))
                sl = x_sb[:, 1024 * tc:1024 * (tc + 1)]
                k_add[tc] = E.op(v, "sv", lambda: v.tensor_add(sl, sl, t_f[:]))
            block.vector(v5)

        st5.close()


def build_moe_standalone(debug_rw=False):
    nc = bass.Bass("TRN2", target_bir_lowering=False)
    X = nc.dram_tensor("X", [T, D], F32, kind="ExternalInput").ap()
    XOUT = nc.dram_tensor("XOUT", [T, D], F32, kind="ExternalOutput").ap()
    w = {
        "GWT": nc.dram_tensor("GWT", [D, 64], F32, kind="ExternalInput").ap(),
        "GB": nc.dram_tensor("GB", [64], F32, kind="ExternalInput").ap(),
        "WGT": nc.dram_tensor("WGT", [64, D, 704], F32, kind="ExternalInput").ap(),
        "WUT": nc.dram_tensor("WUT", [64, D, 704], F32, kind="ExternalInput").ap(),
        "WDT": nc.dram_tensor("WDT", [64, 704, D], F32, kind="ExternalInput").ap(),
        "SGT": nc.dram_tensor("SGT", [D, 1408], F32, kind="ExternalInput").ap(),
        "SUT": nc.dram_tensor("SUT", [D, 1408], F32, kind="ExternalInput").ap(),
        "SDT": nc.dram_tensor("SDT", [1408, D], F32, kind="ExternalInput").ap(),
    }
    if debug_rw:
        w["RWOUT"] = nc.dram_tensor("RWOUT", [T, 64], F32, kind="ExternalOutput").ap()
    st0 = contextlib.ExitStack()
    sh = _mk_shared(nc, st0)
    E = Env(nc, st0)
    with st0, nc.Block() as block:
        FB = FakeBlock(E)
        ld_x, sg_consts = _shared_p0(nc, E, FB, sh, X)
        sh[6] = sg_consts
        emit_moe(nc, E, FB, sh, w, "b_", ld_x)
        E.barrier(FB)
        _emit_store(nc, E, FB, sh[0], XOUT, [])
        FB.replay(block)
    return nc


def build_ffn1_standalone():
    nc = bass.Bass("TRN2", target_bir_lowering=False)
    X = nc.dram_tensor("X", [T, D], F32, kind="ExternalInput").ap()
    XOUT = nc.dram_tensor("XOUT", [T, D], F32, kind="ExternalOutput").ap()
    w = {
        "WRT": nc.dram_tensor("WRT", [D, D], F32, kind="ExternalInput").ap(),
        "WKT": nc.dram_tensor("WKT", [D, D], F32, kind="ExternalInput").ap(),
        "WVT": nc.dram_tensor("WVT", [D, D], F32, kind="ExternalInput").ap(),
        "MUR": nc.dram_tensor("MUR", [D], F32, kind="ExternalInput").ap(),
        "MUK": nc.dram_tensor("MUK", [D], F32, kind="ExternalInput").ap(),
        "SEL": nc.dram_tensor("SEL", [4], F32, kind="ExternalInput").ap(),
        "ccin_h": nc.dram_tensor("ccin_h", [D, 1], BF16).ap(),
        "ccout_h": nc.dram_tensor("ccout_h", [4 * D, 1], BF16).ap(),
    }
    st0 = contextlib.ExitStack()
    sh = _mk_shared(nc, st0)
    E = Env(nc, st0)
    with st0, nc.Block() as block:
        FB = FakeBlock(E)
        ld_x, sg_consts = _shared_p0(nc, E, FB, sh, X)
        sh[6] = sg_consts
        emit_ffn1(nc, E, FB, sh, w, "d_", ld_x)
        E.barrier(FB)
        _emit_store(nc, E, FB, sh[0], XOUT, [])
        FB.replay(block)
    return nc


def build_fused():
    nc = bass.Bass("TRN2", target_bir_lowering=False)
    X = nc.dram_tensor("X", [T, D], F32, kind="ExternalInput").ap()
    XOUT = nc.dram_tensor("XOUT", [T, D], F32, kind="ExternalOutput").ap()
    ein = lambda n, shp: nc.dram_tensor(n, shp, F32, kind="ExternalInput").ap()
    wa = {"WQ": ein("a_WQ", [D, D]), "BQ": ein("a_BQ", [D]),
          "WL": ein("a_WL", [D, 256]), "BL": ein("a_BL", [256]),
          "WK": ein("a_WK", [256, D]), "BK": ein("a_BK", [D]),
          "WV": ein("a_WV", [256, D]), "BV": ein("a_BV", [D]),
          "WO": ein("a_WO", [D, D]), "BO": ein("a_BO", [D]),
          "ccin": nc.dram_tensor("a_ccin", [256, T], BF16).ap(),
          "ccout": nc.dram_tensor("a_ccout", [4 * 256, T], BF16).ap()}
    wb = {"GWT": ein("b_GWT", [D, 64]), "GB": ein("b_GB", [64]),
          "WGT": ein("b_WGT", [64, D, 704]), "WUT": ein("b_WUT", [64, D, 704]),
          "WDT": ein("b_WDT", [64, 704, D]), "SGT": ein("b_SGT", [D, 1408]),
          "SUT": ein("b_SUT", [D, 1408]), "SDT": ein("b_SDT", [1408, D])}
    wc = {"WQ": ein("c_WQ", [D, D]), "WK": ein("c_WK", [D, D]),
          "WV": ein("c_WV", [D, D]), "WO": ein("c_WO", [D, D]),
          "BO": ein("c_BO", [D]),
          "ccin": nc.dram_tensor("c_ccin", [D, T], BF16).ap(),
          "ccout": nc.dram_tensor("c_ccout", [4 * D, T], BF16).ap()}
    wd = {"WRT": ein("d_WRT", [D, D]), "WKT": ein("d_WKT", [D, D]),
          "WVT": ein("d_WVT", [D, D]), "MUR": ein("d_MUR", [D]),
          "MUK": ein("d_MUK", [D]), "SEL": ein("d_SEL", [4]),
          "ccin_h": nc.dram_tensor("d_ccin_h", [D, 1], BF16).ap(),
          "ccout_h": nc.dram_tensor("d_ccout_h", [4 * D, 1], BF16).ap()}
    scr = _attn_scratch_dram(nc)
    wa.update(scr)
    wc.update(scr)

    st0 = contextlib.ExitStack()
    sh = _mk_shared(nc, st0)
    E = Env(nc, st0)
    with st0, nc.Block() as block:
        FB = FakeBlock(E)
        ld_x, sg_consts = _shared_p0(nc, E, FB, sh, X)
        sh[6] = sg_consts
        emit_attn(nc, E, FB, sh, wa, True, "a_", ld_x)
        E.barrier(FB)
        emit_moe(nc, E, FB, sh, wb, "b_", 0)
        E.barrier(FB)
        emit_attn(nc, E, FB, sh, wc, False, "c_", 0)
        E.barrier(FB)
        emit_ffn1(nc, E, FB, sh, wd, "d_", 0)
        E.barrier(FB)
        _emit_store(nc, E, FB, sh[0], XOUT, [])
        FB.replay(block)
    return nc


def _stage_weights(kw):
    f32c = lambda a: np.ascontiguousarray(np.asarray(a, dtype=np.float32))
    w = {}
    for k, s_ in [("a_WQ", "L0_attn_Wq"), ("a_BQ", "L0_attn_bq"),
                  ("a_WL", "L0_attn_Wl"), ("a_BL", "L0_attn_bl"),
                  ("a_WK", "L0_attn_Wk"), ("a_BK", "L0_attn_bk"),
                  ("a_WV", "L0_attn_Wv"), ("a_BV", "L0_attn_bv"),
                  ("a_WO", "L0_attn_Wo"), ("a_BO", "L0_attn_bo"),
                  ("b_GB", "L0_ffn_gate_bias"),
                  ("d_MUR", "L1_ffn_mu_r"), ("d_MUK", "L1_ffn_mu_k")]:
        w[k] = f32c(kw[s_])
    w["b_GWT"] = f32c(np.asarray(kw["L0_ffn_gate_w"], np.float32).T)
    w["b_WGT"] = f32c(np.asarray(kw["L0_ffn_Wg"], np.float32).transpose(0, 2, 1))
    w["b_WUT"] = f32c(np.asarray(kw["L0_ffn_Wu"], np.float32).transpose(0, 2, 1))
    w["b_WDT"] = f32c(np.asarray(kw["L0_ffn_Wd"], np.float32).transpose(0, 2, 1))
    w["b_SGT"] = f32c(np.asarray(kw["L0_ffn_Sg"], np.float32).T)
    w["b_SUT"] = f32c(np.asarray(kw["L0_ffn_Su"], np.float32).T)
    w["b_SDT"] = f32c(np.asarray(kw["L0_ffn_Sd"], np.float32).T)
    w["c_WQ"] = f32c(np.asarray(kw["L1_attn_Wq"], np.float32)
                     .transpose(1, 0, 2).reshape(D, D))
    w["c_WK"] = f32c(np.asarray(kw["L1_attn_Wk"], np.float32)
                     .transpose(1, 0, 2).reshape(D, D))
    w["c_WV"] = f32c(np.asarray(kw["L1_attn_Wv"], np.float32)
                     .transpose(1, 0, 2).reshape(D, D))
    w["c_WO"] = f32c(np.asarray(kw["L1_attn_Wp"], np.float32)
                     .reshape(64, 16, D).transpose(1, 0, 2).reshape(D, D))
    w["c_BO"] = f32c(kw["L1_attn_bp"])
    w["d_WRT"] = f32c(np.asarray(kw["L1_ffn_W_r"], np.float32).T)
    w["d_WKT"] = f32c(np.asarray(kw["L1_ffn_W_k"], np.float32).T)
    w["d_WVT"] = f32c(np.asarray(kw["L1_ffn_W_v"], np.float32).T)
    return w


def kernel_fused(**kw):
    x = np.ascontiguousarray(np.asarray(kw["L0_attn_x"], dtype=np.float32))
    shards = [np.ascontiguousarray(x.reshape(2, 4, T, D)[c // 4, c % 4])
              for c in range(NCORE)]
    cores = list(range(NCORE))
    w = _stage_weights(kw)
    maps = []
    for c in cores:
        sel = np.zeros(4, np.float32)
        if c % 4 > 0:
            sel[c % 4 - 1] = 1.0
        maps.append(dict(w, X=shards[c], d_SEL=sel))
    res = run_bass_kernel_spmd(build_fused(), maps, core_ids=cores)
    out = np.empty((2, SEQ, D), np.float32)
    for c in cores:
        out[c // 4, (c % 4) * T:(c % 4 + 1) * T] =             np.asarray(res.results[c]["XOUT"], np.float32)
    return out
